# version 10
# baseline (speedup 1.0000x reference)
import contextlib
import numpy as np

import concourse.bass as bass
import concourse.mybir as mybir
from concourse.bass_utils import run_bass_kernel_spmd

F32 = mybir.dt.float32
BF16 = mybir.dt.bfloat16
AF = mybir.ActivationFunctionType
ALU = mybir.AluOpType

NCORES = 8
D = 4096
NT = 32
TOK = 1024
NTK = 8
INTER = 11008
NI = 86
RANK = 1024
NR = 8
GF = 6
W1SLOTS = 5
EPS = 1e-5


class Ctr:

    def __init__(self, name):
        self.name = name
        self.val = 0
        self.h = None

    def inc(self, n=1):
        self.val += n
        return self.val


class Prog:
    def __init__(self):
        self.ops = {k: [] for k in ("tensor", "vector", "scalar", "sync", "gpsimd")}

    def emit(self, eng, fn):
        self.ops[eng].append(fn)


def _wait(prog, eng, ctr, val):
    if val > 0:
        prog.emit(eng, lambda e, c=ctr, v=val: e.wait_ge(c.h, v))


def _mask_support(mask):
    return (np.asarray(mask).reshape(NT, 128, NT, 128) != 0).any(axis=(1, 3))


def _runs_in_group(tlist, fg):
    lo, hi = 4 * fg, 4 * fg + 4
    runs = []
    i = 0
    while i < len(tlist):
        t = tlist[i]
        if t < lo or t >= hi:
            i += 1
            continue
        j = i
        while j + 1 < len(tlist) and tlist[j + 1] == tlist[j] + 1 and tlist[j + 1] < hi:
            j += 1
        runs.append((t, j - i + 1, i))
        i = j + 1
    return runs


def _assemble(nc, st, sems, prog):
    with st:
        sem_stack = contextlib.ExitStack()
        with sem_stack:
            for c in sems.values():
                c.h = sem_stack.enter_context(nc.semaphore(c.name))
            with nc.Block() as block:
                block.tensor(lambda e, ops=prog.ops["tensor"]: [f(e) for f in ops])
                block.vector(lambda e, ops=prog.ops["vector"]: [f(e) for f in ops])
                block.scalar(lambda e, ops=prog.ops["scalar"]: [f(e) for f in ops])
                block.sync(lambda e, ops=prog.ops["sync"]: [f(e) for f in ops])
                block.gpsimd(lambda e, ops=prog.ops["gpsimd"]: [f(e) for f in ops])


def build_program(mt, upto="P6"):
    ph = {"PA": 0, "P1": 1, "P3": 2, "PF": 3, "P6": 4}[upto]
    supp_tc = [sorted(int(s) for s in np.nonzero(mt[:, tc])[0]) for tc in range(NT)]
    srow = [sorted(int(t) for t in np.nonzero(mt[s, :])[0]) for s in range(NT)]
    n_tiles = int(mt.sum())
    tiles = [(s, tc) for tc in range(NT) for s in supp_tc[tc]]
    tile_j = {st: j for j, st in enumerate(tiles)}
    sbase = np.concatenate([[0], np.cumsum([len(r) for r in srow])]).astype(int)
    sslot = {}
    for s in range(NT):
        for idx, t in enumerate(srow[s]):
            sslot[(s, t)] = int(sbase[s]) + idx

    uses = {}
    for tc in range(NT):
        for s in supp_tc[tc]:
            uses.setdefault(s, []).append(tc)
    slot_of = {}
    slots = [None] * W1SLOTS
    last_use = {}
    w1_loads_for_tc = {tc: [] for tc in range(NT)}
    load_idx_of = {}
    n_load_events = 0
    use_slot = {}
    use_load_idx = {}

    def next_use(s, tc):
        for u in uses[s]:
            if u >= tc:
                return u
        return 10 ** 9

    for tc in range(NT):
        for s in supp_tc[tc]:
            if s in slot_of:
                continue
            free = [i for i, occ in enumerate(slots) if occ is None]
            if free:
                sl = free[0]
                guard = 0
            else:
                sl = max(range(W1SLOTS), key=lambda i: next_use(slots[i], tc))
                old = slots[sl]
                guard = tile_j[(old, last_use[old])] + 1
                del slot_of[old]
            slots[sl] = s
            slot_of[s] = sl
            n_load_events += 1
            load_idx_of[s] = n_load_events
            w1_loads_for_tc[tc].append((s, sl, guard))
        for s in supp_tc[tc]:
            use_slot[(s, tc)] = slot_of[s]
            use_load_idx[(s, tc)] = load_idx_of[s]
            last_use[s] = tc

    tiles_through = np.cumsum([len(supp_tc[t]) for t in range(NT)]).tolist()

    def mmA_done_tc(t):
        return 0 if t < 0 else int(tiles_through[min(t, NT - 1)])

    groups = []
    i0 = 0
    while i0 < NI:
        groups.append((i0, min(GF, NI - i0)))
        i0 += GF
    n_groups = len(groups)

    nc = bass.Bass("TRN2", target_bir_lowering=False)

    x_h = nc.dram_tensor("x", [TOK, D], F32, kind="ExternalInput").ap()
    w1q_h = nc.dram_tensor("w1q", [NT, 128, NT, 128], F32, kind="ExternalInput").ap()
    w2_h = nc.dram_tensor("w2", [D, D], F32, kind="ExternalInput").ap()
    mk_h = nc.dram_tensor("maskp", [max(n_tiles, 1), 128, 128], F32, kind="ExternalInput").ap()
    u1q_h = nc.dram_tensor("u1q", [128, NT, RANK], F32, kind="ExternalInput").ap()
    v1q_h = nc.dram_tensor("v1q", [128, NR, 2 * INTER], F32, kind="ExternalInput").ap()
    u2_h = nc.dram_tensor("u2", [INTER, RANK], F32, kind="ExternalInput").ap()
    v2q_h = nc.dram_tensor("v2q", [128, NR, D], F32, kind="ExternalInput").ap()
    b1p_h = nc.dram_tensor("b1p", [128, 2 * NI], F32, kind="ExternalInput").ap()
    b2r_h = nc.dram_tensor("b2r", [1, D], F32, kind="ExternalInput").ap()
    b2fr_h = nc.dram_tensor("b2fr", [1, D], F32, kind="ExternalInput").ap()
    ident_h = nc.dram_tensor("ident", [128, 128], F32, kind="ExternalInput").ap()
    out_h = nc.dram_tensor("out", [TOK, D], F32, kind="ExternalOutput").ap()

    prog = Prog()
    st = contextlib.ExitStack()

    xnF_t = st.enter_context(nc.sbuf_tensor("xnF", [128, NT, TOK], BF16))
    A_t = st.enter_context(nc.sbuf_tensor("A_sb", [128, max(n_tiles, 1) * 128], BF16))
    S16_SZ = 38912
    S32_SZ = 8448
    s16_t = st.enter_context(nc.sbuf_tensor("s16", [128, S16_SZ], BF16))
    s32_t = st.enter_context(nc.sbuf_tensor("s32", [128, S32_SZ], F32))
    ones_t = st.enter_context(nc.sbuf_tensor("ones", [1, 128], BF16))
    b1p_t = st.enter_context(nc.sbuf_tensor("b1p_sb", [128, 2 * NI], F32))
    ln_t = st.enter_context(nc.sbuf_tensor("lnsc", [128, 32], F32))
    ident_sb = st.enter_context(nc.sbuf_tensor("ident_sb", [128, 128], BF16))

    xnF = xnF_t[:, :, :].rearrange("p a b -> p a b")
    A_sb = A_t[:, :]
    s16 = s16_t[:, :]
    s32 = s32_t[:, :]

    def sub16(off, shape):
        n = int(np.prod(shape))
        ap = s16[:, off:off + n]
        if len(shape) == 1:
            return ap
        letters = "abcdefg"[: len(shape)]
        spec = "p (" + " ".join(letters) + ") -> p " + " ".join(letters)
        return ap.rearrange(spec, **{letters[i]: shape[i] for i in range(len(shape))})

    def sub32(off, shape):
        n = int(np.prod(shape))
        ap = s32[:, off:off + n]
        if len(shape) == 1:
            return ap
        letters = "abcdefg"[: len(shape)]
        spec = "p (" + " ".join(letters) + ") -> p " + " ".join(letters)
        return ap.rearrange(spec, **{letters[i]: shape[i] for i in range(len(shape))})

    w1win = sub16(0, [W1SLOTS, NT, 128])
    w2bf = sub16(20480, [2, D])
    w2T = sub16(28672, [2, NT, 128])
    U1sb = sub16(36864, [4, 512])
    P_sb = sub16(0, [NR, TOK])
    gate_sb = sub16(8192, [NR, GF * 128])
    up_sb = sub16(14336, [NR, GF * 128])
    act_sb = sub16(20480, [GF, TOK])
    u2g_sb = sub16(26624, [GF, RANK])
    tmp_sb = sub16(36864, [2, 2, 512])
    zbf_sb = sub16(20480, [NR, TOK])
    v2_sb = sub16(8192, [3, NR, 512])
    b2row = sub16(0, [2, D])
    x6_sb = sub32(0, [4, 512])
    t1_sb = sub32(2048, [4, 512])
    o6_sb = sub32(4096, [4, 512])
    mk_sb = sub32(0, [16, 128])
    x1_sb = sub32(2048, [1, D])
    Z_sb = sub32(0, [NR, TOK])
    def ln_slot(par, i):
        return ln_t[:, par * 16 + i:par * 16 + i + 1]
    eps_ap = ln_t[:, 15:16]

    ps = [st.enter_context(nc.psum_tensor(f"ps{i}", [128, 512], F32)) for i in range(4)]
    _psT_stack = contextlib.ExitStack()
    psT_t = _psT_stack.enter_context(nc.psum_tensor("psT", [128, 2, 2048], BF16))
    psT_aps = [[psT_t[:, par, i * 128:(i + 1) * 128] for i in range(16)] for par in range(2)]
    psT_full = [psT_t[:, par, :] for par in range(2)]
    _psT_stack.close()
    ps += [st.enter_context(nc.psum_tensor(f"ps{i}", [128, 512], F32)) for i in range(4, 8)]
    bank_guard = {}

    def use_bank(eng, b):
        g = bank_guard.get(b)
        if g is not None:
            _wait(prog, eng, g[0], g[1])

    def release_bank(b, ctr, val):
        bank_guard[b] = (ctr, val)

    sems = {}

    def S(name):
        if name not in sems:
            sems[name] = Ctr(name)
        return sems[name]

    s_w2h = [[S("w2h00"), S("w2h01")], [S("w2h10"), S("w2h11")]]
    s_w2t, s_mmA, s_aev, s_b1l, s_trp = (
        S("w2t"), S("mmA"), S("aev"), S("b1l"), S("trp"))
    s_mk8 = [S(f"mk{i}") for i in range(8)]
    s_w1s = [S(f"w1s{i}") for i in range(W1SLOTS)]
    s_w1sb = [S(f"w1b{i}") for i in range(W1SLOTS)]
    s_xl0, s_idl = S("xl0"), S("idl")
    s_ag, s_lnr, s_xns = S("ag"), S("lnr"), S("xns")
    s_u1p = [S(f"u1{i}") for i in range(4)]
    s_t3, s_pev, s_pev2 = S("t3"), S("pev"), S("pev2")
    s_v1gp = [[[S(f"v1g{il}_{pp}"), S(f"v1u{il}_{pp}")] for pp in range(2)]
              for il in range(GF)]
    s_u2l, s_gu, s_su, s_act, s_zev, s_f2q = (
        S("u2l"), S("gu"), S("su"), S("act"), S("zev"), S("f2q"))
    s_zbf = S("zbf")
    s_v2p = [S("v2a"), S("v2b"), S("v2c")]
    s_x6p = [S(f"x6{i}") for i in range(4)]
    s_outp = [S(f"out{i}") for i in range(4)]
    s_brl, s_t6y, s_t6o, s_ge, s_oe, s_t1a = (
        S("brl"), S("t6y"), S("t6o"), S("ge"), S("oe"), S("t1a"))

    def _drain(ctr, val, src_ap, extra=()):
        _wait(prog, "gpsimd", ctr, val)
        s_dbg = S("dbg")
        prog.emit("gpsimd", lambda e, src_ap=src_ap: e.dma_start(
            out_h[0:128, 0:512], src_ap).then_inc(s_dbg.h, 16))
        s_dbg.inc(16)
        for (r0, c0, ap) in extra:
            prog.emit("gpsimd", lambda e, r0=r0, c0=c0, ap=ap: e.dma_start(
                out_h[r0:r0 + ap.shape[0], c0:c0 + int(np.prod(ap.shape[1:]))],
                ap).then_inc(s_dbg.h, 16))
            s_dbg.inc(16)
        _wait(prog, "gpsimd", s_dbg, s_dbg.val)
        _assemble(nc, st, sems, prog)
        return nc

    prog.emit("vector", lambda e: e.memset(ones_t[:, :], 1.0).then_inc(s_ag.h, 1))
    s_ag.inc(1)
    prog.emit("vector", lambda e: e.memset(eps_ap, EPS).then_inc(s_ag.h, 1))
    s_ag.inc(1)
    prog.emit("vector", lambda e: e.memset(
        w2T.rearrange("p a b c -> p (a b c)")[:, :], 0.0).then_inc(s_ag.h, 1))
    s_ag.inc(1)

    w1_event_info = [None]

    def emit_w1_load(s, sl, guard_val):
        _wait(prog, "gpsimd", s_mmA, guard_val)
        prog.emit("gpsimd", lambda e, sl=sl, s=s: e.dma_start(
            w1win[:, sl, 0:16, :], w1q_h[s][:, 0:16, :]).then_inc(s_w1s[sl].h, 16))
        s_w1s[sl].inc(16)
        prog.emit("gpsimd", lambda e, sl=sl, s=s: e.dma_start(
            w1win[:, sl, 16:32, :], w1q_h[s][:, 16:32, :]).then_inc(s_w1sb[sl].h, 16))
        s_w1sb[sl].inc(16)
        w1_event_info.append((sl, s_w1s[sl].val, s_w1sb[sl].val))

    prog.emit("gpsimd", lambda e: e.dma_start(ident_sb[:, :], ident_h).then_inc(s_idl.h, 16))
    s_idl.inc(16)

    w2_round_goal = {}
    w2_evac_goal = {}
    xn_evac_goal = {}

    def emit_w2h_load(tc, q):
        _wait(prog, "gpsimd", s_trp, w2_round_goal.get((tc - 2, q), 0))
        prog.emit("gpsimd", lambda e, tc=tc, q=q: e.dma_start(
            w2bf[:, tc % 2, q * 2048:(q + 1) * 2048],
            w2_h[tc * 128:(tc + 1) * 128, q * 2048:(q + 1) * 2048]).then_inc(
                s_w2h[tc % 2][q].h, 16))
        s_w2h[tc % 2][q].inc(16)

    def emit_round(kind, key, q):
        ridx = s_trp.val
        par = ridx % 2
        if ridx == 0:
            _wait(prog, "tensor", s_idl, 16)
        _wait(prog, "tensor", s_w2t, ridx - 1)
        if kind == "w2":
            _wait(prog, "tensor", s_w2h[key % 2][q], 16 * (key // 2 + 1))
            srcs = [w2bf[:, key % 2, (q * 16 + i) * 128:(q * 16 + i + 1) * 128]
                    for i in range(16)]
        else:
            _wait(prog, "tensor", s_xns, key + 1)
            srcs = [xnF[:, q * 16 + i, key * 128:(key + 1) * 128] for i in range(16)]
        for i in range(16):
            if i == 15:
                prog.emit("tensor", lambda e, par=par, i=i, sap=srcs[i]: e.transpose(
                    psT_aps[par][i], sap, ident_sb[:, :]).then_inc(s_trp.h, 1))
            else:
                prog.emit("tensor", lambda e, par=par, i=i, sap=srcs[i]: e.transpose(
                    psT_aps[par][i], sap, ident_sb[:, :]))
        s_trp.inc(1)
        if ridx == 0:
            _wait(prog, "vector", s_ag, 3)
        _wait(prog, "vector", s_trp, s_trp.val)
        if kind == "w2":
            dst = w2T[:, key % 2, q * 16:(q + 1) * 16, :].rearrange("p a b -> p (a b)")
        else:
            dst = xnF[:, q * 16:(q + 1) * 16, key * 128:(key + 1) * 128]
        prog.emit("vector", lambda e, par=par, dst=dst: e.tensor_tensor(
            dst, psT_full[par], dst, ALU.bypass).then_inc(s_w2t.h, 1))
        s_w2t.inc(1)
        if kind == "w2":
            w2_round_goal[(key, q)] = s_trp.val
            w2_evac_goal[(key, q)] = s_w2t.val
        else:
            xn_evac_goal[key] = s_w2t.val

    def emit_ln_chunk(tk):
        par = tk % 2
        stage = xnF[:, :, tk * 128:(tk + 1) * 128]
        sx, negm, sxx, lv, rstdv = [ln_slot(par, i) for i in range(5)]
        if tk == 0:
            _wait(prog, "scalar", s_ag, 3)
            prog.emit("scalar", lambda e: e.dma_start(
                x1_sb[:, 0, :], x_h[0:128, :]).then_inc(s_xl0.h, 16))
            s_xl0.inc(16)
        _wait(prog, "scalar", s_xl0, 16 * (tk + 1))
        base = s_lnr.val

        def _lnw(idx):
            if idx > 0:
                _wait(prog, "scalar", s_lnr, base + idx)
        _lnw(0)
        prog.emit("scalar", lambda e, par=par, sx=sx, stage=stage: e.activation(
            stage, x1_sb[:, 0, :], AF.Identity, accum_out=sx).then_inc(s_lnr.h, 1))
        s_lnr.inc(1)
        _lnw(1)
        prog.emit("scalar", lambda e, negm=negm, sx=sx: e.activation(
            negm, sx, AF.Identity, scale=-1.0 / D).then_inc(s_lnr.h, 1))
        s_lnr.inc(1)
        _lnw(2)
        prog.emit("scalar", lambda e, par=par, negm=negm: e.activation(
            x1_sb[:, 0, :], x1_sb[:, 0, :], AF.Identity,
            bias=negm).then_inc(s_lnr.h, 1))
        s_lnr.inc(1)
        _lnw(3)
        prog.emit("scalar", lambda e, par=par, sxx=sxx, stage=stage: e.activation(
            stage, x1_sb[:, 0, :], AF.Square, accum_out=sxx).then_inc(s_lnr.h, 1))
        s_lnr.inc(1)
        _lnw(4)
        prog.emit("scalar", lambda e, lv=lv, sxx=sxx: e.activation(
            lv, sxx, AF.Ln, scale=1.0 / D, bias=eps_ap).then_inc(s_lnr.h, 1))
        s_lnr.inc(1)
        _lnw(5)
        _wait(prog, "scalar", s_xns, tk - 1)
        prog.emit("scalar", lambda e, rstdv=rstdv, lv=lv: e.activation(
            rstdv, lv, AF.Exp, scale=-0.5).then_inc(s_lnr.h, 1))
        s_lnr.inc(1)
        _lnw(6)
        prog.emit("scalar", lambda e, par=par, stage=stage, rstdv=rstdv: e.activation(
            stage, x1_sb[:, 0, :], AF.Copy, scale=rstdv).then_inc(s_xns.h, 1))
        s_xns.inc(1)
        if tk + 1 < NTK:
            _wait(prog, "scalar", s_xns, tk + 1)
            prog.emit("scalar", lambda e, t=tk + 1: e.dma_start(
                x1_sb[:, 0, :], x_h[t * 128:(t + 1) * 128, :]).then_inc(s_xl0.h, 16))
            s_xl0.inc(16)

    for tk in range(NTK):
        emit_ln_chunk(tk)

    j = 0
    for tc in range(NT):
        for s in supp_tc[tc]:
            _wait(prog, "sync", s_aev, j - 15)
            _wait(prog, "sync", s_mk8[j % 8], 16 * (j // 8))
            prog.emit("sync", lambda e, j=j: e.dma_start(
                mk_sb[:, j % 16, :], mk_h[j]).then_inc(s_mk8[j % 8].h, 16))
            s_mk8[j % 8].inc(16)
            j += 1

    XN_AT = {16 + 2 * tk: tk for tk in range(NTK)}
    for q in range(2):
        emit_w2h_load(0, q)
    for s, sl, gv in w1_loads_for_tc[0]:
        emit_w1_load(s, sl, gv)
    for q in range(2):
        emit_w2h_load(1, q)
    emit_round("w2", 0, 0)
    emit_round("w2", 0, 1)
    for q in range(2):
        emit_w2h_load(2, q)
    prog.emit("gpsimd", lambda e: e.dma_start(b1p_t[:, :], b1p_h).then_inc(s_b1l.h, 16))
    s_b1l.inc(16)

    j = 0
    for tc in range(NT):
        if tc in XN_AT:
            emit_round("xn", XN_AT[tc], 0)
            emit_round("xn", XN_AT[tc], 1)
        if tc + 1 < NT:
            emit_round("w2", tc + 1, 0)
            emit_round("w2", tc + 1, 1)
            for s, sl, gv in w1_loads_for_tc[tc + 1]:
                emit_w1_load(s, sl, gv)
            if tc + 3 < NT:
                for q in range(2):
                    emit_w2h_load(tc + 3, q)
        _wait(prog, "tensor", s_w2t, w2_evac_goal[(tc, 1)])
        for s in supp_tc[tc]:
            sl_w, v_a, v_b = w1_event_info[use_load_idx[(s, tc)]]
            _wait(prog, "tensor", s_w1s[sl_w], v_a)
            _wait(prog, "tensor", s_w1sb[sl_w], v_b)
            b = j % 4
            use_bank("tensor", b)
            sl = use_slot[(s, tc)]
            for d in range(NT):
                if d < NT - 1:
                    prog.emit("tensor", lambda e, b=b, sl=sl, d=d, par=tc % 2, st0=(d == 0): e.matmul(
                        ps[b][:, 0:128], w1win[:, sl, d, :], w2T[:, par, d, :],
                        start=st0, stop=False))
                else:
                    prog.emit("tensor", lambda e, b=b, sl=sl, d=d, par=tc % 2: e.matmul(
                        ps[b][:, 0:128], w1win[:, sl, d, :], w2T[:, par, d, :],
                        start=False, stop=True).then_inc(s_mmA.h, 1))
            s_mmA.inc(1)
            release_bank(b, s_aev, j + 1)
            j += 1
        for s in supp_tc[tc]:
            ja = tile_j[(s, tc)]
            _wait(prog, "vector", s_mmA, ja + 1)
            _wait(prog, "vector", s_mk8[ja % 8], 16 * (ja // 8 + 1))
            slot = sslot[(s, tc)]
            prog.emit("vector", lambda e, b=ja % 4, slot=slot, ja=ja: e.tensor_tensor(
                A_sb[:, slot * 128:(slot + 1) * 128], ps[b][:, 0:128],
                mk_sb[:, ja % 16, :], ALU.mult).then_inc(s_aev.h, 1))
            s_aev.inc(1)

    for b in range(4, 8):
        release_bank(b, s_w2t, s_w2t.val)

    if ph == 0:
        return _drain(s_aev, n_tiles, A_sb[:, 0:512])

    if ph == 1:
        return _drain(s_w2t, s_w2t.val, xnF[:, 0, 0:512],
                      extra=[(128, 0, ln_t[:, :])])

    v1_goal = {}
    g0_i0, g0_gsz = groups[0]
    v1_chain = []

    def emit_v1_piece(g, il, which, guard_ctr, guard_val):
        i = groups[g][0] + il
        _wait(prog, "gpsimd", guard_ctr, guard_val)
        sem = s_v1gp[il][g % 2][0 if which == "gate" else 1]
        _wait(prog, "gpsimd", sem, sem.val)
        if len(v1_chain) >= 4:
            csem, cval = v1_chain[-4]
            _wait(prog, "gpsimd", csem, cval)
        if which == "gate":
            prog.emit("gpsimd", lambda e, il=il, i=i, sem=sem: e.dma_start(
                gate_sb[:, :, il * 128:(il + 1) * 128],
                v1q_h[:, :, i * 128:(i + 1) * 128]).then_inc(sem.h, 16))
        else:
            prog.emit("gpsimd", lambda e, il=il, i=i, sem=sem: e.dma_start(
                up_sb[:, :, il * 128:(il + 1) * 128],
                v1q_h[:, :, INTER + i * 128:INTER + (i + 1) * 128]).then_inc(sem.h, 16))
        sem.inc(16)
        v1_goal[(g, il, which)] = (sem, sem.val)
        v1_chain.append((sem, sem.val))

    g0_pieces = [(il, w) for il in range(g0_gsz) for w in ("gate", "up")]
    g0_at = {6 + 2 * n: pc for n, pc in enumerate(g0_pieces)}
    for p in range(2):
        for h in range(NT):
            k = p * NT + h
            _wait(prog, "gpsimd", s_t3, k - 3)
            prog.emit("gpsimd", lambda e, p=p, h=h, sl=k % 4: e.dma_start(
                U1sb[:, sl, :], u1q_h[:, h, p * 512:(p + 1) * 512]).then_inc(s_u1p[sl].h, 16))
            s_u1p[k % 4].inc(16)
            if k in g0_at:
                il, w = g0_at[k]
                emit_v1_piece(0, il, w, s_mmA, n_tiles)
    for p in range(2):
        for h in range(NT):
            k = p * NT + h
            if k == 0:
                _wait(prog, "tensor", s_w2t, xn_evac_goal[NTK - 1])
            _wait(prog, "tensor", s_u1p[k % 4], 16 * (k // 4 + 1))
            if h == 0:
                for b in range(8):
                    use_bank("tensor", b)
            for rl in range(4):
                for hf in range(2):
                    b = rl * 2 + hf
                    last = (rl == 3) and (hf == 1)
                    if last:
                        prog.emit("tensor", lambda e, b=b, rl=rl, hf=hf, h=h, sl=k % 4, st0=(h == 0), sp=(h == NT - 1): e.matmul(
                            ps[b][:, :], U1sb[:, sl, rl * 128:(rl + 1) * 128],
                            xnF[:, h, hf * 512:(hf + 1) * 512], start=st0, stop=sp).then_inc(s_t3.h, 1))
                    else:
                        prog.emit("tensor", lambda e, b=b, rl=rl, hf=hf, h=h, sl=k % 4, st0=(h == 0), sp=(h == NT - 1): e.matmul(
                            ps[b][:, :], U1sb[:, sl, rl * 128:(rl + 1) * 128],
                            xnF[:, h, hf * 512:(hf + 1) * 512], start=st0, stop=sp))
            s_t3.inc(1)
        for rl in range(4):
            for hf in range(2):
                b = rl * 2 + hf
                eng = "vector" if rl < 2 else "scalar"
                _wait(prog, eng, s_t3, NT * (p + 1))
                psem = s_pev if eng == "vector" else s_pev2
                if eng == "vector":
                    prog.emit("vector", lambda e, b=b, r=p * 4 + rl, hf=hf, psem=psem: e.tensor_tensor(
                        P_sb[:, r, hf * 512:hf * 512 + 512], ps[b][:, :],
                        xnF[:, 0, 0:512], ALU.bypass).then_inc(psem.h, 1))
                else:
                    prog.emit("scalar", lambda e, b=b, r=p * 4 + rl, hf=hf, psem=psem: e.activation(
                        P_sb[:, r, hf * 512:hf * 512 + 512], ps[b][:, :],
                        AF.Identity).then_inc(psem.h, 1))
                psem.inc(1)
                release_bank(b, psem, psem.val)

    if ph == 2:
        return _drain(s_pev, 8, P_sb[:, 0, 0:512])

    u2_cum = 0
    zev_goal_prev = 0
    tmp_guard = {0: 0, 1: 0}
    f1_goal = []
    f2_goal = []
    for g, (i0, gsz) in enumerate(groups):
        gu_base = s_gu.val
        su_base = s_su.val
        act_base = s_act.val
        gu_val_of = {(il, hf): gu_base + il * 4 + hf * 2 + 2 for il in range(gsz) for hf in range(2)}
        su_val_of = {(il, hf): su_base + (il * 2 + hf) * 2 + 2 for il in range(gsz) for hf in range(2)}
        act_val_of = {(il, hf): act_base + il * 2 + hf + 1 for il in range(gsz) for hf in range(2)}

        if g > 0:
            for il in range(gsz):
                if il < prev_gsz:
                    gg = prev_gu_base + il * 4 + 3
                    gu_ = prev_gu_base + il * 4 + 4
                else:
                    gg = gu_ = f1_goal[g - 1]
                emit_v1_piece(g, il, "gate", s_gu, gg)
                emit_v1_piece(g, il, "up", s_gu, gu_)
        if g == 0:
            _wait(prog, "gpsimd", s_mmA, n_tiles)
        for il in range(gsz):
            _wait(prog, "gpsimd", s_zev, zev_goal_prev)
            prog.emit("gpsimd", lambda e, i=i0 + il, il=il: e.dma_start(
                u2g_sb[:, il, :], u2_h[i * 128:(i + 1) * 128, :]).then_inc(s_u2l.h, 16))
            s_u2l.inc(16)
            u2_cum += 1
        prev_gu_base, prev_gsz = gu_base, gsz

        for il in range(gsz):
            for w_ in ("gate", "up"):
                sem, v = v1_goal[(g, il, w_)]
                _wait(prog, "tensor", sem, v)
            if g == 0 and il == 0:
                _wait(prog, "tensor", s_pev, 8)
                _wait(prog, "tensor", s_pev2, 8)
            for hf in range(2):
                for gu in range(2):
                    b = 4 * (il % 2) + 2 * hf + gu
                    use_bank("tensor", b)
                    src = gate_sb if gu == 0 else up_sb
                    for r in range(NR):
                        if r == NR - 1:
                            prog.emit("tensor", lambda e, b=b, src=src, r=r, il=il, hf=hf: e.matmul(
                                ps[b][:, :], src[:, r, il * 128:(il + 1) * 128],
                                P_sb[:, r, hf * 512:hf * 512 + 512],
                                start=False, stop=True).then_inc(s_gu.h, 1))
                        else:
                            prog.emit("tensor", lambda e, b=b, src=src, r=r, il=il, hf=hf, st0=(r == 0): e.matmul(
                                ps[b][:, :], src[:, r, il * 128:(il + 1) * 128],
                                P_sb[:, r, hf * 512:hf * 512 + 512],
                                start=st0, stop=False))
                    s_gu.inc(1)
                    release_bank(b, s_su, su_val_of[(il, hf)] - (1 if gu == 0 else 0))
        f1_goal.append(s_gu.val)

        for il in range(gsz):
            gi = i0 + il
            for hf in range(2):
                bg = 4 * (il % 2) + 2 * hf + 0
                bu = 4 * (il % 2) + 2 * hf + 1
                par = il % 2
                _wait(prog, "scalar", s_gu, gu_val_of[(il, hf)])
                if g == 0 and il == 0 and hf == 0:
                    _wait(prog, "scalar", s_pev, 8)
                    _wait(prog, "scalar", s_pev2, 8)
                    _wait(prog, "scalar", s_b1l, 16)
                _wait(prog, "scalar", s_act, tmp_guard[par])
                tmp_guard[par] = act_val_of[(il, hf)]
                prog.emit("scalar", lambda e, bg=bg, par=par, col=gi: e.activation(
                    tmp_sb[:, par, 0, :], ps[bg][:, :], AF.Silu,
                    bias=b1p_t[:, col:col + 1]).then_inc(s_su.h, 1))
                s_su.inc(1)
                prog.emit("scalar", lambda e, bu=bu, par=par, col=NI + gi: e.activation(
                    tmp_sb[:, par, 1, :], ps[bu][:, :], AF.Identity,
                    bias=b1p_t[:, col:col + 1]).then_inc(s_su.h, 1))
                s_su.inc(1)

        for il in range(gsz):
            for hf in range(2):
                if il == 0 and hf == 0 and g > 0:
                    _wait(prog, "vector", s_f2q, f2_goal[g - 1])
                _wait(prog, "vector", s_su, su_val_of[(il, hf)])
                prog.emit("vector", lambda e, il=il, hf=hf, par=il % 2: e.tensor_tensor(
                    act_sb[:, il, hf * 512:hf * 512 + 512], tmp_sb[:, par, 0, :],
                    tmp_sb[:, par, 1, :], ALU.mult).then_inc(s_act.h, 1))
                s_act.inc(1)

        _wait(prog, "tensor", s_u2l, 16 * u2_cum)
        for r in range(NR):
            for hf in range(2):
                b = (2 * r + hf) % 8
                use_bank("tensor", b)
                for il in range(gsz):
                    _wait(prog, "tensor", s_act, act_val_of[(il, hf)])
                    last = (il == gsz - 1)
                    if last:
                        prog.emit("tensor", lambda e, b=b, r=r, il=il, hf=hf: e.matmul(
                            ps[b][:, :], u2g_sb[:, il, r * 128:(r + 1) * 128],
                            act_sb[:, il, hf * 512:hf * 512 + 512],
                            start=False, stop=True).then_inc(s_f2q.h, 1))
                    else:
                        prog.emit("tensor", lambda e, b=b, r=r, il=il, hf=hf, st0=(il == 0): e.matmul(
                            ps[b][:, :], u2g_sb[:, il, r * 128:(r + 1) * 128],
                            act_sb[:, il, hf * 512:hf * 512 + 512],
                            start=st0, stop=False))
                s_f2q.inc(1)
                _wait(prog, "vector", s_f2q, s_f2q.val)
                if g == 0:
                    if r == 0 and hf == 0:
                        _wait(prog, "vector", s_aev, n_tiles)
                    prog.emit("vector", lambda e, b=b, r=r, hf=hf: e.tensor_tensor(
                        Z_sb[:, r, hf * 512:hf * 512 + 512], ps[b][:, :],
                        xnF[:, 0, 0:512], ALU.bypass).then_inc(s_zev.h, 1))
                else:
                    prog.emit("vector", lambda e, b=b, r=r, hf=hf: e.tensor_tensor(
                        Z_sb[:, r, hf * 512:hf * 512 + 512], ps[b][:, :],
                        Z_sb[:, r, hf * 512:hf * 512 + 512], ALU.add).then_inc(s_zev.h, 1))
                s_zev.inc(1)
                release_bank(b, s_zev, s_zev.val)
        zev_goal_prev = s_zev.val
        f2_goal.append(s_f2q.val)

    for eng in ("vector", "scalar"):
        _wait(prog, eng, s_f2q, f2_goal[-1])
        _wait(prog, eng, s_zev, s_zev.val)
    for r in range(NR):
        if r < 4:
            prog.emit("vector", lambda e, r=r: e.tensor_copy(
                zbf_sb[:, r, :], Z_sb[:, r, :]).then_inc(s_zbf.h, 1))
        else:
            prog.emit("scalar", lambda e, r=r: e.activation(
                zbf_sb[:, r, :], Z_sb[:, r, :], AF.Identity).then_inc(s_zbf.h, 1))
        s_zbf.inc(1)

    if ph == 3:
        return _drain(s_zbf, NR, zbf_sb[:, 0, 0:512])

    _wait(prog, "gpsimd", s_gu, f1_goal[-1])
    prog.emit("gpsimd", lambda e: e.dma_start(b2row[0:1, 0, :], b2r_h).then_inc(s_brl.h, 16))
    s_brl.inc(16)
    prog.emit("gpsimd", lambda e: e.dma_start(b2row[0:1, 1, :], b2fr_h).then_inc(s_brl.h, 16))
    s_brl.inc(16)

    def emit_v2_load(fg):
        if fg >= 3:
            _wait(prog, "gpsimd", s_t6o, NTK * (fg - 2))
        prog.emit("gpsimd", lambda e, fg=fg: e.dma_start(
            v2_sb[:, fg % 3, :, :], v2q_h[:, :, fg * 512:(fg + 1) * 512]).then_inc(s_v2p[fg % 3].h, 16))
        s_v2p[fg % 3].inc(16)

    def emit_x6(k):
        fg, tk = divmod(k, NTK)
        if k == 0:
            _wait(prog, "scalar", s_zbf, NR)
        _wait(prog, "scalar", s_oe, k - 3)
        prog.emit("scalar", lambda e, fg=fg, tk=tk, par=k % 4: e.dma_start(
            x6_sb[:, par, :], x_h[tk * 128:(tk + 1) * 128, fg * 512:(fg + 1) * 512]).then_inc(s_x6p[par].h, 16))
        s_x6p[k % 4].inc(16)

    emit_v2_load(0)
    emit_v2_load(1)
    emit_v2_load(2)
    emit_x6(0)
    for k in range(64):
        fg, tk = divmod(k, NTK)
        if k == 0:
            _wait(prog, "tensor", s_brl, 32)
            _wait(prog, "tensor", s_aev, n_tiles)
        if tk == 0:
            _wait(prog, "tensor", s_v2p[fg % 3], 16 * (fg // 3 + 1))
        bY = 2 * (k % 2)
        bO = bY + 1
        runs_all = []
        for s in range(NT):
            for (t0, ln, idx) in _runs_in_group(srow[s], fg):
                runs_all.append((s, t0, ln, int(sbase[s]) + idx))
        use_bank("tensor", bY)
        if not runs_all:
            prog.emit("tensor", lambda e, bY=bY, fg=fg: e.matmul(
                ps[bY][:, :], ones_t[0:1, 0:128], b2row[0:1, 0, fg * 512:(fg + 1) * 512],
                start=True, stop=True).then_inc(s_t6y.h, 1))
        else:
            prog.emit("tensor", lambda e, bY=bY, fg=fg: e.matmul(
                ps[bY][:, :], ones_t[0:1, 0:128], b2row[0:1, 0, fg * 512:(fg + 1) * 512],
                start=True, stop=False))
        for q, (s, t0, ln, slot0) in enumerate(runs_all):
            last = (q == len(runs_all) - 1)
            trel = t0 - 4 * fg
            if last:
                prog.emit("tensor", lambda e, bY=bY, s=s, tk=tk, trel=trel, ln=ln, slot0=slot0: e.matmul(
                    ps[bY][:, trel * 128:(trel + ln) * 128], xnF[:, s, tk * 128:(tk + 1) * 128],
                    A_sb[:, slot0 * 128:(slot0 + ln) * 128],
                    start=False, stop=True).then_inc(s_t6y.h, 1))
            else:
                prog.emit("tensor", lambda e, bY=bY, s=s, tk=tk, trel=trel, ln=ln, slot0=slot0: e.matmul(
                    ps[bY][:, trel * 128:(trel + ln) * 128], xnF[:, s, tk * 128:(tk + 1) * 128],
                    A_sb[:, slot0 * 128:(slot0 + ln) * 128],
                    start=False, stop=False))
        s_t6y.inc(1)
        use_bank("tensor", bO)
        if k == 0:
            _wait(prog, "tensor", s_zbf, NR)
        prog.emit("tensor", lambda e, bO=bO, fg=fg: e.matmul(
            ps[bO][:, :], ones_t[0:1, 0:128], b2row[0:1, 1, fg * 512:(fg + 1) * 512],
            start=True, stop=False))
        for r in range(NR):
            if r == NR - 1:
                prog.emit("tensor", lambda e, bO=bO, r=r, tk=tk, fg=fg: e.matmul(
                    ps[bO][:, :], zbf_sb[:, r, tk * 128:(tk + 1) * 128],
                    v2_sb[:, fg % 3, r, :], start=False, stop=True).then_inc(s_t6o.h, 1))
            else:
                prog.emit("tensor", lambda e, bO=bO, r=r, tk=tk, fg=fg: e.matmul(
                    ps[bO][:, :], zbf_sb[:, r, tk * 128:(tk + 1) * 128],
                    v2_sb[:, fg % 3, r, :], start=False, stop=False))
        s_t6o.inc(1)

        if tk == 0 and fg + 3 < 8:
            emit_v2_load(fg + 3)

        if k + 1 < 64:
            emit_x6(k + 1)
        _wait(prog, "scalar", s_t6y, k + 1)
        prog.emit("scalar", lambda e, bY=bY, par=k % 4: e.activation(
            t1_sb[:, par, :], ps[bY][:, :], AF.Gelu).then_inc(s_ge.h, 1))
        s_ge.inc(1)
        release_bank(bY, s_ge, s_ge.val)

        _wait(prog, "vector", s_ge, k + 1)
        _wait(prog, "vector", s_t6o, k + 1)
        _wait(prog, "vector", s_x6p[k % 4], 16 * (k // 4 + 1))
        prog.emit("vector", lambda e, bO=bO, par=k % 4: e.tensor_tensor(
            t1_sb[:, par, :], ps[bO][:, :], t1_sb[:, par, :], ALU.add).then_inc(s_t1a.h, 1))
        s_t1a.inc(1)
        _wait(prog, "vector", s_t1a, s_t1a.val)
        if k >= 4:
            _wait(prog, "vector", s_outp[k % 4], 16 * (k // 4))
        prog.emit("vector", lambda e, par=k % 4: e.tensor_tensor(
            o6_sb[:, par, :], t1_sb[:, par, :], x6_sb[:, par, :], ALU.add).then_inc(s_oe.h, 1))
        s_oe.inc(1)
        release_bank(bO, s_oe, s_oe.val)

        _wait(prog, "sync", s_oe, k + 1)
        prog.emit("sync", lambda e, fg=fg, tk=tk, par=k % 4: e.dma_start(
            out_h[tk * 128:(tk + 1) * 128, fg * 512:(fg + 1) * 512],
            o6_sb[:, par, :]).then_inc(s_outp[par].h, 16))
        s_outp[k % 4].inc(16)
    for i in range(4):
        _wait(prog, "sync", s_outp[i], s_outp[i].val)

    _assemble(nc, st, sems, prog)
    return nc


def kernel(L0_attn_x, L0_attn_w1, L0_attn_w2, L0_attn_b2, L0_attn_sparse_mask,
           L0_ffn_U1, L0_ffn_V1, L0_ffn_U2, L0_ffn_V2, L0_ffn_b1, L0_ffn_b2):
    x = np.ascontiguousarray(np.asarray(L0_attn_x, dtype=np.float32).reshape(2 * D, D))
    w1 = np.asarray(L0_attn_w1, dtype=np.float32)
    w2 = np.ascontiguousarray(np.asarray(L0_attn_w2, dtype=np.float32))
    b2 = np.asarray(L0_attn_b2, dtype=np.float32)
    mask = np.asarray(L0_attn_sparse_mask, dtype=np.float32)
    U1 = np.asarray(L0_ffn_U1, dtype=np.float32)
    V1 = np.asarray(L0_ffn_V1, dtype=np.float32)
    U2 = np.ascontiguousarray(np.asarray(L0_ffn_U2, dtype=np.float32))
    V2 = np.asarray(L0_ffn_V2, dtype=np.float32)
    b1 = np.asarray(L0_ffn_b1, dtype=np.float32)
    b2f = np.asarray(L0_ffn_b2, dtype=np.float32)

    mt = _mask_support(mask)
    tiles = [(s, tc) for tc in range(NT) for s in sorted(np.nonzero(mt[:, tc])[0])]
    n_tiles = len(tiles)
    maskp = np.ascontiguousarray(np.stack(
        [mask[s * 128:(s + 1) * 128, t * 128:(t + 1) * 128] for (s, t) in tiles])) \
        if n_tiles else np.zeros((1, 128, 128), np.float32)

    w1q = np.ascontiguousarray(w1.reshape(NT, 128, NT, 128).transpose(2, 1, 0, 3))
    u1q = np.ascontiguousarray(U1.reshape(NT, 128, RANK).transpose(1, 0, 2))
    v1q = np.ascontiguousarray(V1.reshape(NR, 128, 2 * INTER).transpose(1, 0, 2))
    v2q = np.ascontiguousarray(V2.reshape(NR, 128, D).transpose(1, 0, 2))
    b1g = b1[:INTER].reshape(NI, 128).T
    b1u = b1[INTER:].reshape(NI, 128).T
    b1p = np.ascontiguousarray(np.concatenate([b1g, b1u], axis=1))
    b2r = np.ascontiguousarray(b2.reshape(1, D))
    b2fr = np.ascontiguousarray(b2f.reshape(1, D))
    ident = np.eye(128, dtype=np.float32)

    nc = build_program(mt)
    in_maps = []
    for c in range(NCORES):
        in_maps.append({
            "x": x[c * TOK:(c + 1) * TOK],
            "w1q": w1q, "w2": w2, "maskp": maskp,
            "u1q": u1q, "v1q": v1q, "u2": U2, "v2q": v2q,
            "b1p": b1p, "b2r": b2r, "b2fr": b2fr, "ident": ident,
        })
    res = run_bass_kernel_spmd(nc, in_maps, core_ids=list(range(NCORES)))
    out = np.concatenate([np.asarray(r["out"]) for r in res.results], axis=0)
    return out.reshape(2, D, D).astype(np.float32)


# version 12
# speedup vs baseline: 1.0379x; 1.0379x over previous
import contextlib
import numpy as np

import concourse.bass as bass
import concourse.mybir as mybir
from concourse.bass_utils import run_bass_kernel_spmd

F32 = mybir.dt.float32
BF16 = mybir.dt.bfloat16
AF = mybir.ActivationFunctionType
ALU = mybir.AluOpType

NCORES = 8
D = 4096
NT = 32
TOK = 1024
NTK = 8
INTER = 11008
NI = 86
RANK = 1024
NR = 8
GF = 6
W1SLOTS = 5
EPS = 1e-5


class Ctr:

    def __init__(self, name):
        self.name = name
        self.val = 0
        self.h = None

    def inc(self, n=1):
        self.val += n
        return self.val


class Prog:
    def __init__(self):
        self.ops = {k: [] for k in ("tensor", "vector", "scalar", "sync", "gpsimd")}

    def emit(self, eng, fn):
        self.ops[eng].append(fn)


def _wait(prog, eng, ctr, val):
    if val > 0:
        prog.emit(eng, lambda e, c=ctr, v=val: e.wait_ge(c.h, v))


def _mask_support(mask):
    return (np.asarray(mask).reshape(NT, 128, NT, 128) != 0).any(axis=(1, 3))


def _runs_in_group(tlist, fg):
    lo, hi = 4 * fg, 4 * fg + 4
    runs = []
    i = 0
    while i < len(tlist):
        t = tlist[i]
        if t < lo or t >= hi:
            i += 1
            continue
        j = i
        while j + 1 < len(tlist) and tlist[j + 1] == tlist[j] + 1 and tlist[j + 1] < hi:
            j += 1
        runs.append((t, j - i + 1, i))
        i = j + 1
    return runs


def _assemble(nc, st, sems, prog):
    with st:
        sem_stack = contextlib.ExitStack()
        with sem_stack:
            for c in sems.values():
                c.h = sem_stack.enter_context(nc.semaphore(c.name))
            with nc.Block() as block:
                block.tensor(lambda e, ops=prog.ops["tensor"]: [f(e) for f in ops])
                block.vector(lambda e, ops=prog.ops["vector"]: [f(e) for f in ops])
                block.scalar(lambda e, ops=prog.ops["scalar"]: [f(e) for f in ops])
                block.sync(lambda e, ops=prog.ops["sync"]: [f(e) for f in ops])
                block.gpsimd(lambda e, ops=prog.ops["gpsimd"]: [f(e) for f in ops])


def build_program(mt, upto="P6"):
    ph = {"PA": 0, "P1": 1, "P3": 2, "PF": 3, "P6": 4}[upto]
    supp_tc = [sorted(int(s) for s in np.nonzero(mt[:, tc])[0]) for tc in range(NT)]
    srow = [sorted(int(t) for t in np.nonzero(mt[s, :])[0]) for s in range(NT)]
    n_tiles = int(mt.sum())
    tiles = [(s, tc) for tc in range(NT) for s in supp_tc[tc]]
    tile_j = {st: j for j, st in enumerate(tiles)}
    sbase = np.concatenate([[0], np.cumsum([len(r) for r in srow])]).astype(int)
    sslot = {}
    for s in range(NT):
        for idx, t in enumerate(srow[s]):
            sslot[(s, t)] = int(sbase[s]) + idx

    uses = {}
    for tc in range(NT):
        for s in supp_tc[tc]:
            uses.setdefault(s, []).append(tc)
    slot_of = {}
    slots = [None] * W1SLOTS
    last_use = {}
    w1_loads_for_tc = {tc: [] for tc in range(NT)}
    load_idx_of = {}
    n_load_events = 0
    use_slot = {}
    use_load_idx = {}

    def next_use(s, tc):
        for u in uses[s]:
            if u >= tc:
                return u
        return 10 ** 9

    for tc in range(NT):
        for s in supp_tc[tc]:
            if s in slot_of:
                continue
            free = [i for i, occ in enumerate(slots) if occ is None]
            if free:
                sl = free[0]
                guard = 0
            else:
                sl = max(range(W1SLOTS), key=lambda i: next_use(slots[i], tc))
                old = slots[sl]
                guard = tile_j[(old, last_use[old])] + 1
                del slot_of[old]
            slots[sl] = s
            slot_of[s] = sl
            n_load_events += 1
            load_idx_of[s] = n_load_events
            w1_loads_for_tc[tc].append((s, sl, guard))
        for s in supp_tc[tc]:
            use_slot[(s, tc)] = slot_of[s]
            use_load_idx[(s, tc)] = load_idx_of[s]
            last_use[s] = tc

    tiles_through = np.cumsum([len(supp_tc[t]) for t in range(NT)]).tolist()

    def mmA_done_tc(t):
        return 0 if t < 0 else int(tiles_through[min(t, NT - 1)])

    groups = []
    i0 = 0
    while i0 < NI:
        groups.append((i0, min(GF, NI - i0)))
        i0 += GF
    n_groups = len(groups)

    nc = bass.Bass("TRN2", target_bir_lowering=False)

    x_h = nc.dram_tensor("x", [TOK, D], F32, kind="ExternalInput").ap()
    w1q_h = nc.dram_tensor("w1q", [NT, 128, NT, 128], F32, kind="ExternalInput").ap()
    w2_h = nc.dram_tensor("w2", [D, D], F32, kind="ExternalInput").ap()
    mk_h = nc.dram_tensor("maskp", [max(n_tiles, 1), 128, 128], F32, kind="ExternalInput").ap()
    u1q_h = nc.dram_tensor("u1q", [128, NT, RANK], F32, kind="ExternalInput").ap()
    v1q_h = nc.dram_tensor("v1q", [128, NR, 2 * INTER], F32, kind="ExternalInput").ap()
    u2_h = nc.dram_tensor("u2", [INTER, RANK], F32, kind="ExternalInput").ap()
    v2q_h = nc.dram_tensor("v2q", [128, NR, D], F32, kind="ExternalInput").ap()
    b1p_h = nc.dram_tensor("b1p", [128, 2 * NI], F32, kind="ExternalInput").ap()
    b2r_h = nc.dram_tensor("b2r", [1, D], F32, kind="ExternalInput").ap()
    b2fr_h = nc.dram_tensor("b2fr", [1, D], F32, kind="ExternalInput").ap()
    ident_h = nc.dram_tensor("ident", [128, 128], F32, kind="ExternalInput").ap()
    out_h = nc.dram_tensor("out", [TOK, D], F32, kind="ExternalOutput").ap()

    prog = Prog()
    st = contextlib.ExitStack()

    xnF_t = st.enter_context(nc.sbuf_tensor("xnF", [128, NT, TOK], BF16))
    A_t = st.enter_context(nc.sbuf_tensor("A_sb", [128, max(n_tiles, 1) * 128], BF16))
    S16_SZ = 38912
    S32_SZ = 8448
    s16_t = st.enter_context(nc.sbuf_tensor("s16", [128, S16_SZ], BF16))
    s32_t = st.enter_context(nc.sbuf_tensor("s32", [128, S32_SZ], F32))
    ones_t = st.enter_context(nc.sbuf_tensor("ones", [1, 128], BF16))
    b1p_t = st.enter_context(nc.sbuf_tensor("b1p_sb", [128, 2 * NI], F32))
    ln_t = st.enter_context(nc.sbuf_tensor("lnsc", [128, 32], F32))
    ident_sb = st.enter_context(nc.sbuf_tensor("ident_sb", [128, 128], BF16))

    xnF = xnF_t[:, :, :].rearrange("p a b -> p a b")
    A_sb = A_t[:, :]
    s16 = s16_t[:, :]
    s32 = s32_t[:, :]

    def sub16(off, shape):
        n = int(np.prod(shape))
        ap = s16[:, off:off + n]
        if len(shape) == 1:
            return ap
        letters = "abcdefg"[: len(shape)]
        spec = "p (" + " ".join(letters) + ") -> p " + " ".join(letters)
        return ap.rearrange(spec, **{letters[i]: shape[i] for i in range(len(shape))})

    def sub32(off, shape):
        n = int(np.prod(shape))
        ap = s32[:, off:off + n]
        if len(shape) == 1:
            return ap
        letters = "abcdefg"[: len(shape)]
        spec = "p (" + " ".join(letters) + ") -> p " + " ".join(letters)
        return ap.rearrange(spec, **{letters[i]: shape[i] for i in range(len(shape))})

    w1win = sub16(0, [W1SLOTS, NT, 128])
    w2bf = sub16(20480, [2, D])
    w2T = sub16(28672, [2, NT, 128])
    U1sb = sub16(36864, [4, 512])
    P_sb = sub16(0, [NR, TOK])
    gate_sb = sub16(8192, [NR, GF * 128])
    up_sb = sub16(14336, [NR, GF * 128])
    act_sb = sub16(20480, [GF, TOK])
    u2g_sb = sub16(26624, [GF, RANK])
    tmp_sb = sub16(36864, [2, 2, 512])
    zbf_sb = sub16(20480, [NR, TOK])
    v2_sb = sub16(8192, [3, NR, 512])
    b2row = sub16(0, [2, D])
    x6_sb = sub32(0, [4, 512])
    t1_sb = sub32(2048, [4, 512])
    o6_sb = sub32(4096, [4, 512])
    mk_sb = sub32(0, [16, 128])
    x1_sb = sub32(2048, [1, D])
    Z_sb = sub32(0, [NR, TOK])
    def ln_slot(par, i):
        return ln_t[:, par * 16 + i:par * 16 + i + 1]
    eps_ap = ln_t[:, 15:16]

    ps = [st.enter_context(nc.psum_tensor(f"ps{i}", [128, 512], F32)) for i in range(4)]
    _psT_stack = contextlib.ExitStack()
    psT_t = _psT_stack.enter_context(nc.psum_tensor("psT", [128, 2, 2048], BF16))
    psT_aps = [[psT_t[:, par, i * 128:(i + 1) * 128] for i in range(16)] for par in range(2)]
    psT_full = [psT_t[:, par, :] for par in range(2)]
    _psT_stack.close()
    ps += [st.enter_context(nc.psum_tensor(f"ps{i}", [128, 512], F32)) for i in range(4, 8)]
    bank_guard = {}

    def use_bank(eng, b):
        g = bank_guard.get(b)
        if g is not None:
            _wait(prog, eng, g[0], g[1])

    def release_bank(b, ctr, val):
        bank_guard[b] = (ctr, val)

    sems = {}

    def S(name):
        if name not in sems:
            sems[name] = Ctr(name)
        return sems[name]

    s_w2h = [[S("w2h00"), S("w2h01")], [S("w2h10"), S("w2h11")]]
    s_w2t, s_mmA, s_aev, s_b1l, s_trp = (
        S("w2t"), S("mmA"), S("aev"), S("b1l"), S("trp"))
    s_mk8 = [S(f"mk{i}") for i in range(8)]
    s_w1s = [S(f"w1s{i}") for i in range(W1SLOTS)]
    s_xl0, s_idl = S("xl0"), S("idl")
    s_ag, s_lnr, s_xns = S("ag"), S("lnr"), S("xns")
    s_u1p = [S(f"u1{i}") for i in range(4)]
    s_t3, s_pev, s_pev2 = S("t3"), S("pev"), S("pev2")
    s_v1gp = [[[S(f"v1g{il}_{pp}"), S(f"v1u{il}_{pp}")] for pp in range(2)]
              for il in range(GF)]
    s_u2l, s_gu, s_su, s_act, s_zev, s_f2q = (
        S("u2l"), S("gu"), S("su"), S("act"), S("zev"), S("f2q"))
    s_zbf = S("zbf")
    s_v2p = [S("v2a"), S("v2b"), S("v2c")]
    s_x6p = [S(f"x6{i}") for i in range(4)]
    s_outp = [S(f"out{i}") for i in range(4)]
    s_brl, s_t6y, s_t6o, s_ge, s_oe, s_t1a = (
        S("brl"), S("t6y"), S("t6o"), S("ge"), S("oe"), S("t1a"))

    def _drain(ctr, val, src_ap, extra=()):
        _wait(prog, "gpsimd", ctr, val)
        s_dbg = S("dbg")
        prog.emit("gpsimd", lambda e, src_ap=src_ap: e.dma_start(
            out_h[0:128, 0:512], src_ap).then_inc(s_dbg.h, 16))
        s_dbg.inc(16)
        for (r0, c0, ap) in extra:
            prog.emit("gpsimd", lambda e, r0=r0, c0=c0, ap=ap: e.dma_start(
                out_h[r0:r0 + ap.shape[0], c0:c0 + int(np.prod(ap.shape[1:]))],
                ap).then_inc(s_dbg.h, 16))
            s_dbg.inc(16)
        _wait(prog, "gpsimd", s_dbg, s_dbg.val)
        _assemble(nc, st, sems, prog)
        return nc

    prog.emit("vector", lambda e: e.memset(ones_t[:, :], 1.0).then_inc(s_ag.h, 1))
    s_ag.inc(1)
    prog.emit("vector", lambda e: e.memset(eps_ap, EPS).then_inc(s_ag.h, 1))
    s_ag.inc(1)
    prog.emit("vector", lambda e: e.memset(
        w2T.rearrange("p a b c -> p (a b c)")[:, :], 0.0).then_inc(s_ag.h, 1))
    s_ag.inc(1)

    w1_event_info = [None]

    def emit_w1_load(s, sl, guard_val):
        _wait(prog, "gpsimd", s_mmA, guard_val)
        prog.emit("gpsimd", lambda e, sl=sl, s=s: e.dma_start(
            w1win[:, sl, :, :], w1q_h[s]).then_inc(s_w1s[sl].h, 16))
        s_w1s[sl].inc(16)
        w1_event_info.append((sl, s_w1s[sl].val))

    prog.emit("gpsimd", lambda e: e.dma_start(ident_sb[:, :], ident_h).then_inc(s_idl.h, 16))
    s_idl.inc(16)

    w2_round_goal = {}
    w2_evac_goal = {}
    xn_evac_goal = {}

    def emit_w2h_load(tc, q):
        _wait(prog, "gpsimd", s_trp, w2_round_goal.get((tc - 2, q), 0))
        prog.emit("gpsimd", lambda e, tc=tc, q=q: e.dma_start(
            w2bf[:, tc % 2, q * 2048:(q + 1) * 2048],
            w2_h[tc * 128:(tc + 1) * 128, q * 2048:(q + 1) * 2048]).then_inc(
                s_w2h[tc % 2][q].h, 16))
        s_w2h[tc % 2][q].inc(16)

    def emit_round(kind, key, q):
        ridx = s_trp.val
        par = ridx % 2
        if ridx == 0:
            _wait(prog, "tensor", s_idl, 16)
        _wait(prog, "tensor", s_w2t, ridx - 1)
        if kind == "w2":
            _wait(prog, "tensor", s_w2h[key % 2][q], 16 * (key // 2 + 1))
            srcs = [w2bf[:, key % 2, (q * 16 + i) * 128:(q * 16 + i + 1) * 128]
                    for i in range(16)]
        else:
            _wait(prog, "tensor", s_xns, key + 1)
            srcs = [xnF[:, q * 16 + i, key * 128:(key + 1) * 128] for i in range(16)]
        for i in range(16):
            if i == 15:
                prog.emit("tensor", lambda e, par=par, i=i, sap=srcs[i]: e.transpose(
                    psT_aps[par][i], sap, ident_sb[:, :]).then_inc(s_trp.h, 1))
            else:
                prog.emit("tensor", lambda e, par=par, i=i, sap=srcs[i]: e.transpose(
                    psT_aps[par][i], sap, ident_sb[:, :]))
        s_trp.inc(1)
        if ridx == 0:
            _wait(prog, "vector", s_ag, 3)
        _wait(prog, "vector", s_trp, s_trp.val)
        if kind == "w2":
            dst = w2T[:, key % 2, q * 16:(q + 1) * 16, :].rearrange("p a b -> p (a b)")
        else:
            dst = xnF[:, q * 16:(q + 1) * 16, key * 128:(key + 1) * 128]
        prog.emit("vector", lambda e, par=par, dst=dst: e.tensor_tensor(
            dst, psT_full[par], dst, ALU.bypass).then_inc(s_w2t.h, 1))
        s_w2t.inc(1)
        if kind == "w2":
            w2_round_goal[(key, q)] = s_trp.val
            w2_evac_goal[(key, q)] = s_w2t.val
        else:
            xn_evac_goal[key] = s_w2t.val

    def emit_ln_chunk(tk):
        par = tk % 2
        stage = xnF[:, :, tk * 128:(tk + 1) * 128]
        sx, negm, sxx, lv, rstdv = [ln_slot(par, i) for i in range(5)]
        if tk == 0:
            _wait(prog, "scalar", s_ag, 3)
            prog.emit("scalar", lambda e: e.dma_start(
                x1_sb[:, 0, :], x_h[0:128, :]).then_inc(s_xl0.h, 16))
            s_xl0.inc(16)
        _wait(prog, "scalar", s_xl0, 16 * (tk + 1))
        base = s_lnr.val

        def _lnw(idx):
            if idx > 0:
                _wait(prog, "scalar", s_lnr, base + idx)
        _lnw(0)
        prog.emit("scalar", lambda e, par=par, sx=sx, stage=stage: e.activation(
            stage, x1_sb[:, 0, :], AF.Identity, accum_out=sx).then_inc(s_lnr.h, 1))
        s_lnr.inc(1)
        _lnw(1)
        prog.emit("scalar", lambda e, negm=negm, sx=sx: e.activation(
            negm, sx, AF.Identity, scale=-1.0 / D).then_inc(s_lnr.h, 1))
        s_lnr.inc(1)
        _lnw(2)
        prog.emit("scalar", lambda e, par=par, negm=negm: e.activation(
            x1_sb[:, 0, :], x1_sb[:, 0, :], AF.Identity,
            bias=negm).then_inc(s_lnr.h, 1))
        s_lnr.inc(1)
        _lnw(3)
        prog.emit("scalar", lambda e, par=par, sxx=sxx, stage=stage: e.activation(
            stage, x1_sb[:, 0, :], AF.Square, accum_out=sxx).then_inc(s_lnr.h, 1))
        s_lnr.inc(1)
        _lnw(4)
        prog.emit("scalar", lambda e, lv=lv, sxx=sxx: e.activation(
            lv, sxx, AF.Ln, scale=1.0 / D, bias=eps_ap).then_inc(s_lnr.h, 1))
        s_lnr.inc(1)
        _lnw(5)
        _wait(prog, "scalar", s_xns, tk - 1)
        prog.emit("scalar", lambda e, rstdv=rstdv, lv=lv: e.activation(
            rstdv, lv, AF.Exp, scale=-0.5).then_inc(s_lnr.h, 1))
        s_lnr.inc(1)
        _lnw(6)
        prog.emit("scalar", lambda e, par=par, stage=stage, rstdv=rstdv: e.activation(
            stage, x1_sb[:, 0, :], AF.Copy, scale=rstdv).then_inc(s_xns.h, 1))
        s_xns.inc(1)
        if tk + 1 < NTK:
            _wait(prog, "scalar", s_xns, tk + 1)
            prog.emit("scalar", lambda e, t=tk + 1: e.dma_start(
                x1_sb[:, 0, :], x_h[t * 128:(t + 1) * 128, :]).then_inc(s_xl0.h, 16))
            s_xl0.inc(16)

    for tk in range(NTK):
        emit_ln_chunk(tk)

    j = 0
    for tc in range(NT):
        for s in supp_tc[tc]:
            _wait(prog, "sync", s_aev, j - 15)
            _wait(prog, "sync", s_mk8[j % 8], 16 * (j // 8))
            prog.emit("sync", lambda e, j=j: e.dma_start(
                mk_sb[:, j % 16, :], mk_h[j]).then_inc(s_mk8[j % 8].h, 16))
            s_mk8[j % 8].inc(16)
            j += 1

    XN_AT = {3 + 2 * tk: tk for tk in range(NTK)}
    for q in range(2):
        emit_w2h_load(0, q)
    for s, sl, gv in w1_loads_for_tc[0]:
        emit_w1_load(s, sl, gv)
    for q in range(2):
        emit_w2h_load(1, q)
    emit_round("w2", 0, 0)
    emit_round("w2", 0, 1)
    for q in range(2):
        emit_w2h_load(2, q)
    prog.emit("gpsimd", lambda e: e.dma_start(b1p_t[:, :], b1p_h).then_inc(s_b1l.h, 16))
    s_b1l.inc(16)

    j = 0
    for tc in range(NT):
        if tc in XN_AT:
            emit_round("xn", XN_AT[tc], 0)
            emit_round("xn", XN_AT[tc], 1)
        if tc + 1 < NT:
            emit_round("w2", tc + 1, 0)
            emit_round("w2", tc + 1, 1)
            for s, sl, gv in w1_loads_for_tc[tc + 1]:
                emit_w1_load(s, sl, gv)
            if tc + 3 < NT:
                for q in range(2):
                    emit_w2h_load(tc + 3, q)
        _wait(prog, "tensor", s_w2t, w2_evac_goal[(tc, 1)])
        for s in supp_tc[tc]:
            sl_w, v_a = w1_event_info[use_load_idx[(s, tc)]]
            _wait(prog, "tensor", s_w1s[sl_w], v_a)
            b = j % 4
            use_bank("tensor", b)
            sl = use_slot[(s, tc)]
            for d in range(NT):
                if d < NT - 1:
                    prog.emit("tensor", lambda e, b=b, sl=sl, d=d, par=tc % 2, st0=(d == 0): e.matmul(
                        ps[b][:, 0:128], w1win[:, sl, d, :], w2T[:, par, d, :],
                        start=st0, stop=False))
                else:
                    prog.emit("tensor", lambda e, b=b, sl=sl, d=d, par=tc % 2: e.matmul(
                        ps[b][:, 0:128], w1win[:, sl, d, :], w2T[:, par, d, :],
                        start=False, stop=True).then_inc(s_mmA.h, 1))
            s_mmA.inc(1)
            release_bank(b, s_aev, j + 1)
            j += 1
        for s in supp_tc[tc]:
            ja = tile_j[(s, tc)]
            _wait(prog, "vector", s_mmA, ja + 1)
            _wait(prog, "vector", s_mk8[ja % 8], 16 * (ja // 8 + 1))
            slot = sslot[(s, tc)]
            prog.emit("vector", lambda e, b=ja % 4, slot=slot, ja=ja: e.tensor_tensor(
                A_sb[:, slot * 128:(slot + 1) * 128], ps[b][:, 0:128],
                mk_sb[:, ja % 16, :], ALU.mult).then_inc(s_aev.h, 1))
            s_aev.inc(1)

    for b in range(4, 8):
        release_bank(b, s_w2t, s_w2t.val)

    if ph == 0:
        return _drain(s_aev, n_tiles, A_sb[:, 0:512])

    if ph == 1:
        return _drain(s_w2t, s_w2t.val, xnF[:, 0, 0:512],
                      extra=[(128, 0, ln_t[:, :])])

    v1_goal = {}
    g0_i0, g0_gsz = groups[0]
    v1_chain = []

    def emit_v1_piece(g, il, which, guard_ctr, guard_val):
        i = groups[g][0] + il
        _wait(prog, "gpsimd", guard_ctr, guard_val)
        sem = s_v1gp[il][g % 2][0 if which == "gate" else 1]
        _wait(prog, "gpsimd", sem, sem.val)
        if len(v1_chain) >= 4:
            csem, cval = v1_chain[-4]
            _wait(prog, "gpsimd", csem, cval)
        if which == "gate":
            prog.emit("gpsimd", lambda e, il=il, i=i, sem=sem: e.dma_start(
                gate_sb[:, :, il * 128:(il + 1) * 128],
                v1q_h[:, :, i * 128:(i + 1) * 128]).then_inc(sem.h, 16))
        else:
            prog.emit("gpsimd", lambda e, il=il, i=i, sem=sem: e.dma_start(
                up_sb[:, :, il * 128:(il + 1) * 128],
                v1q_h[:, :, INTER + i * 128:INTER + (i + 1) * 128]).then_inc(sem.h, 16))
        sem.inc(16)
        v1_goal[(g, il, which)] = (sem, sem.val)
        v1_chain.append((sem, sem.val))

    g0_pieces = [(il, w) for il in range(g0_gsz) for w in ("gate", "up")]
    g0_at = {6 + 2 * n: pc for n, pc in enumerate(g0_pieces)}
    for p in range(2):
        for h in range(NT):
            k = p * NT + h
            _wait(prog, "gpsimd", s_t3, k - 3)
            prog.emit("gpsimd", lambda e, p=p, h=h, sl=k % 4: e.dma_start(
                U1sb[:, sl, :], u1q_h[:, h, p * 512:(p + 1) * 512]).then_inc(s_u1p[sl].h, 16))
            s_u1p[k % 4].inc(16)
            if k in g0_at:
                il, w = g0_at[k]
                emit_v1_piece(0, il, w, s_mmA, n_tiles)
    for p in range(2):
        for h in range(NT):
            k = p * NT + h
            if k == 0:
                _wait(prog, "tensor", s_w2t, xn_evac_goal[NTK - 1])
            _wait(prog, "tensor", s_u1p[k % 4], 16 * (k // 4 + 1))
            if h == 0:
                for b in range(8):
                    use_bank("tensor", b)
            for rl in range(4):
                for hf in range(2):
                    b = rl * 2 + hf
                    last = (rl == 3) and (hf == 1)
                    if last:
                        prog.emit("tensor", lambda e, b=b, rl=rl, hf=hf, h=h, sl=k % 4, st0=(h == 0), sp=(h == NT - 1): e.matmul(
                            ps[b][:, :], U1sb[:, sl, rl * 128:(rl + 1) * 128],
                            xnF[:, h, hf * 512:(hf + 1) * 512], start=st0, stop=sp).then_inc(s_t3.h, 1))
                    else:
                        prog.emit("tensor", lambda e, b=b, rl=rl, hf=hf, h=h, sl=k % 4, st0=(h == 0), sp=(h == NT - 1): e.matmul(
                            ps[b][:, :], U1sb[:, sl, rl * 128:(rl + 1) * 128],
                            xnF[:, h, hf * 512:(hf + 1) * 512], start=st0, stop=sp))
            s_t3.inc(1)
        for rl in range(4):
            for hf in range(2):
                b = rl * 2 + hf
                eng = "vector" if rl < 2 else "scalar"
                _wait(prog, eng, s_t3, NT * (p + 1))
                psem = s_pev if eng == "vector" else s_pev2
                if eng == "vector":
                    prog.emit("vector", lambda e, b=b, r=p * 4 + rl, hf=hf, psem=psem: e.tensor_tensor(
                        P_sb[:, r, hf * 512:hf * 512 + 512], ps[b][:, :],
                        xnF[:, 0, 0:512], ALU.bypass).then_inc(psem.h, 1))
                else:
                    prog.emit("scalar", lambda e, b=b, r=p * 4 + rl, hf=hf, psem=psem: e.activation(
                        P_sb[:, r, hf * 512:hf * 512 + 512], ps[b][:, :],
                        AF.Identity).then_inc(psem.h, 1))
                psem.inc(1)
                release_bank(b, psem, psem.val)

    if ph == 2:
        return _drain(s_pev, 8, P_sb[:, 0, 0:512])

    u2_cum = 0
    zev_goal_prev = 0
    tmp_guard = {0: 0, 1: 0}
    f1_goal = []
    f2_goal = []
    for g, (i0, gsz) in enumerate(groups):
        gu_base = s_gu.val
        su_base = s_su.val
        act_base = s_act.val
        gu_val_of = {(il, hf): gu_base + il * 4 + hf * 2 + 2 for il in range(gsz) for hf in range(2)}
        su_val_of = {(il, hf): su_base + (il * 2 + hf) * 2 + 2 for il in range(gsz) for hf in range(2)}
        act_val_of = {(il, hf): act_base + il * 2 + hf + 1 for il in range(gsz) for hf in range(2)}

        if g > 0:
            for il in range(gsz):
                if il < prev_gsz:
                    gg = prev_gu_base + il * 4 + 3
                    gu_ = prev_gu_base + il * 4 + 4
                else:
                    gg = gu_ = f1_goal[g - 1]
                emit_v1_piece(g, il, "gate", s_gu, gg)
                emit_v1_piece(g, il, "up", s_gu, gu_)
        if g == 0:
            _wait(prog, "gpsimd", s_mmA, n_tiles)
        for il in range(gsz):
            _wait(prog, "gpsimd", s_zev, zev_goal_prev)
            prog.emit("gpsimd", lambda e, i=i0 + il, il=il: e.dma_start(
                u2g_sb[:, il, :], u2_h[i * 128:(i + 1) * 128, :]).then_inc(s_u2l.h, 16))
            s_u2l.inc(16)
            u2_cum += 1
        prev_gu_base, prev_gsz = gu_base, gsz

        for il in range(gsz):
            for w_ in ("gate", "up"):
                sem, v = v1_goal[(g, il, w_)]
                _wait(prog, "tensor", sem, v)
            if g == 0 and il == 0:
                _wait(prog, "tensor", s_pev, 8)
                _wait(prog, "tensor", s_pev2, 8)
            for hf in range(2):
                for gu in range(2):
                    b = 4 * (il % 2) + 2 * hf + gu
                    use_bank("tensor", b)
                    src = gate_sb if gu == 0 else up_sb
                    for r in range(NR):
                        if r == NR - 1:
                            prog.emit("tensor", lambda e, b=b, src=src, r=r, il=il, hf=hf: e.matmul(
                                ps[b][:, :], src[:, r, il * 128:(il + 1) * 128],
                                P_sb[:, r, hf * 512:hf * 512 + 512],
                                start=False, stop=True).then_inc(s_gu.h, 1))
                        else:
                            prog.emit("tensor", lambda e, b=b, src=src, r=r, il=il, hf=hf, st0=(r == 0): e.matmul(
                                ps[b][:, :], src[:, r, il * 128:(il + 1) * 128],
                                P_sb[:, r, hf * 512:hf * 512 + 512],
                                start=st0, stop=False))
                    s_gu.inc(1)
                    release_bank(b, s_su, su_val_of[(il, hf)] - (1 if gu == 0 else 0))
        f1_goal.append(s_gu.val)

        for il in range(gsz):
            gi = i0 + il
            for hf in range(2):
                bg = 4 * (il % 2) + 2 * hf + 0
                bu = 4 * (il % 2) + 2 * hf + 1
                par = il % 2
                _wait(prog, "scalar", s_gu, gu_val_of[(il, hf)])
                if g == 0 and il == 0 and hf == 0:
                    _wait(prog, "scalar", s_pev, 8)
                    _wait(prog, "scalar", s_pev2, 8)
                    _wait(prog, "scalar", s_b1l, 16)
                _wait(prog, "scalar", s_act, tmp_guard[par])
                tmp_guard[par] = act_val_of[(il, hf)]
                prog.emit("scalar", lambda e, bg=bg, par=par, col=gi: e.activation(
                    tmp_sb[:, par, 0, :], ps[bg][:, :], AF.Silu,
                    bias=b1p_t[:, col:col + 1]).then_inc(s_su.h, 1))
                s_su.inc(1)
                prog.emit("scalar", lambda e, bu=bu, par=par, col=NI + gi: e.activation(
                    tmp_sb[:, par, 1, :], ps[bu][:, :], AF.Identity,
                    bias=b1p_t[:, col:col + 1]).then_inc(s_su.h, 1))
                s_su.inc(1)

        for il in range(gsz):
            for hf in range(2):
                if il == 0 and hf == 0 and g > 0:
                    _wait(prog, "vector", s_f2q, f2_goal[g - 1])
                _wait(prog, "vector", s_su, su_val_of[(il, hf)])
                prog.emit("vector", lambda e, il=il, hf=hf, par=il % 2: e.tensor_tensor(
                    act_sb[:, il, hf * 512:hf * 512 + 512], tmp_sb[:, par, 0, :],
                    tmp_sb[:, par, 1, :], ALU.mult).then_inc(s_act.h, 1))
                s_act.inc(1)

        _wait(prog, "tensor", s_u2l, 16 * u2_cum)
        for r in range(NR):
            for hf in range(2):
                b = (2 * r + hf) % 8
                use_bank("tensor", b)
                for il in range(gsz):
                    _wait(prog, "tensor", s_act, act_val_of[(il, hf)])
                    last = (il == gsz - 1)
                    if last:
                        prog.emit("tensor", lambda e, b=b, r=r, il=il, hf=hf: e.matmul(
                            ps[b][:, :], u2g_sb[:, il, r * 128:(r + 1) * 128],
                            act_sb[:, il, hf * 512:hf * 512 + 512],
                            start=False, stop=True).then_inc(s_f2q.h, 1))
                    else:
                        prog.emit("tensor", lambda e, b=b, r=r, il=il, hf=hf, st0=(il == 0): e.matmul(
                            ps[b][:, :], u2g_sb[:, il, r * 128:(r + 1) * 128],
                            act_sb[:, il, hf * 512:hf * 512 + 512],
                            start=st0, stop=False))
                s_f2q.inc(1)
                _wait(prog, "vector", s_f2q, s_f2q.val)
                if g == 0:
                    if r == 0 and hf == 0:
                        _wait(prog, "vector", s_aev, n_tiles)
                    prog.emit("vector", lambda e, b=b, r=r, hf=hf: e.tensor_tensor(
                        Z_sb[:, r, hf * 512:hf * 512 + 512], ps[b][:, :],
                        xnF[:, 0, 0:512], ALU.bypass).then_inc(s_zev.h, 1))
                else:
                    prog.emit("vector", lambda e, b=b, r=r, hf=hf: e.tensor_tensor(
                        Z_sb[:, r, hf * 512:hf * 512 + 512], ps[b][:, :],
                        Z_sb[:, r, hf * 512:hf * 512 + 512], ALU.add).then_inc(s_zev.h, 1))
                s_zev.inc(1)
                release_bank(b, s_zev, s_zev.val)
        zev_goal_prev = s_zev.val
        f2_goal.append(s_f2q.val)

    for eng in ("vector", "scalar"):
        _wait(prog, eng, s_f2q, f2_goal[-1])
        _wait(prog, eng, s_zev, s_zev.val)
    for r in range(NR):
        if r < 4:
            prog.emit("vector", lambda e, r=r: e.tensor_copy(
                zbf_sb[:, r, :], Z_sb[:, r, :]).then_inc(s_zbf.h, 1))
        else:
            prog.emit("scalar", lambda e, r=r: e.activation(
                zbf_sb[:, r, :], Z_sb[:, r, :], AF.Identity).then_inc(s_zbf.h, 1))
        s_zbf.inc(1)

    if ph == 3:
        return _drain(s_zbf, NR, zbf_sb[:, 0, 0:512])

    _wait(prog, "gpsimd", s_gu, f1_goal[-1])
    prog.emit("gpsimd", lambda e: e.dma_start(b2row[0:1, 0, :], b2r_h).then_inc(s_brl.h, 16))
    s_brl.inc(16)
    prog.emit("gpsimd", lambda e: e.dma_start(b2row[0:1, 1, :], b2fr_h).then_inc(s_brl.h, 16))
    s_brl.inc(16)

    def emit_v2_load(fg):
        if fg >= 3:
            _wait(prog, "gpsimd", s_t6o, NTK * (fg - 2))
        prog.emit("gpsimd", lambda e, fg=fg: e.dma_start(
            v2_sb[:, fg % 3, :, :], v2q_h[:, :, fg * 512:(fg + 1) * 512]).then_inc(s_v2p[fg % 3].h, 16))
        s_v2p[fg % 3].inc(16)

    def emit_x6(k):
        fg, tk = divmod(k, NTK)
        if k == 0:
            _wait(prog, "scalar", s_zbf, NR)
        _wait(prog, "scalar", s_oe, k - 3)
        prog.emit("scalar", lambda e, fg=fg, tk=tk, par=k % 4: e.dma_start(
            x6_sb[:, par, :], x_h[tk * 128:(tk + 1) * 128, fg * 512:(fg + 1) * 512]).then_inc(s_x6p[par].h, 16))
        s_x6p[k % 4].inc(16)

    emit_v2_load(0)
    emit_v2_load(1)
    emit_v2_load(2)
    emit_x6(0)
    for k in range(64):
        fg, tk = divmod(k, NTK)
        if k == 0:
            _wait(prog, "tensor", s_brl, 32)
            _wait(prog, "tensor", s_aev, n_tiles)
        if tk == 0:
            _wait(prog, "tensor", s_v2p[fg % 3], 16 * (fg // 3 + 1))
        bY = 2 * (k % 2)
        bO = bY + 1
        runs_all = []
        for s in range(NT):
            for (t0, ln, idx) in _runs_in_group(srow[s], fg):
                runs_all.append((s, t0, ln, int(sbase[s]) + idx))
        use_bank("tensor", bY)
        if not runs_all:
            prog.emit("tensor", lambda e, bY=bY, fg=fg: e.matmul(
                ps[bY][:, :], ones_t[0:1, 0:128], b2row[0:1, 0, fg * 512:(fg + 1) * 512],
                start=True, stop=True).then_inc(s_t6y.h, 1))
        else:
            prog.emit("tensor", lambda e, bY=bY, fg=fg: e.matmul(
                ps[bY][:, :], ones_t[0:1, 0:128], b2row[0:1, 0, fg * 512:(fg + 1) * 512],
                start=True, stop=False))
        for q, (s, t0, ln, slot0) in enumerate(runs_all):
            last = (q == len(runs_all) - 1)
            trel = t0 - 4 * fg
            if last:
                prog.emit("tensor", lambda e, bY=bY, s=s, tk=tk, trel=trel, ln=ln, slot0=slot0: e.matmul(
                    ps[bY][:, trel * 128:(trel + ln) * 128], xnF[:, s, tk * 128:(tk + 1) * 128],
                    A_sb[:, slot0 * 128:(slot0 + ln) * 128],
                    start=False, stop=True).then_inc(s_t6y.h, 1))
            else:
                prog.emit("tensor", lambda e, bY=bY, s=s, tk=tk, trel=trel, ln=ln, slot0=slot0: e.matmul(
                    ps[bY][:, trel * 128:(trel + ln) * 128], xnF[:, s, tk * 128:(tk + 1) * 128],
                    A_sb[:, slot0 * 128:(slot0 + ln) * 128],
                    start=False, stop=False))
        s_t6y.inc(1)
        use_bank("tensor", bO)
        if k == 0:
            _wait(prog, "tensor", s_zbf, NR)
        prog.emit("tensor", lambda e, bO=bO, fg=fg: e.matmul(
            ps[bO][:, :], ones_t[0:1, 0:128], b2row[0:1, 1, fg * 512:(fg + 1) * 512],
            start=True, stop=False))
        for r in range(NR):
            if r == NR - 1:
                prog.emit("tensor", lambda e, bO=bO, r=r, tk=tk, fg=fg: e.matmul(
                    ps[bO][:, :], zbf_sb[:, r, tk * 128:(tk + 1) * 128],
                    v2_sb[:, fg % 3, r, :], start=False, stop=True).then_inc(s_t6o.h, 1))
            else:
                prog.emit("tensor", lambda e, bO=bO, r=r, tk=tk, fg=fg: e.matmul(
                    ps[bO][:, :], zbf_sb[:, r, tk * 128:(tk + 1) * 128],
                    v2_sb[:, fg % 3, r, :], start=False, stop=False))
        s_t6o.inc(1)

        if tk == 0 and fg + 3 < 8:
            emit_v2_load(fg + 3)

        if k + 1 < 64:
            emit_x6(k + 1)
        _wait(prog, "scalar", s_t6y, k + 1)
        prog.emit("scalar", lambda e, bY=bY, par=k % 4: e.activation(
            t1_sb[:, par, :], ps[bY][:, :], AF.Gelu).then_inc(s_ge.h, 1))
        s_ge.inc(1)
        release_bank(bY, s_ge, s_ge.val)

        _wait(prog, "vector", s_ge, k + 1)
        _wait(prog, "vector", s_t6o, k + 1)
        _wait(prog, "vector", s_x6p[k % 4], 16 * (k // 4 + 1))
        prog.emit("vector", lambda e, bO=bO, par=k % 4: e.tensor_tensor(
            t1_sb[:, par, :], ps[bO][:, :], t1_sb[:, par, :], ALU.add).then_inc(s_t1a.h, 1))
        s_t1a.inc(1)
        _wait(prog, "vector", s_t1a, s_t1a.val)
        if k >= 4:
            _wait(prog, "vector", s_outp[k % 4], 16 * (k // 4))
        prog.emit("vector", lambda e, par=k % 4: e.tensor_tensor(
            o6_sb[:, par, :], t1_sb[:, par, :], x6_sb[:, par, :], ALU.add).then_inc(s_oe.h, 1))
        s_oe.inc(1)
        release_bank(bO, s_oe, s_oe.val)

        _wait(prog, "sync", s_oe, k + 1)
        prog.emit("sync", lambda e, fg=fg, tk=tk, par=k % 4: e.dma_start(
            out_h[tk * 128:(tk + 1) * 128, fg * 512:(fg + 1) * 512],
            o6_sb[:, par, :]).then_inc(s_outp[par].h, 16))
        s_outp[k % 4].inc(16)
    for i in range(4):
        _wait(prog, "sync", s_outp[i], s_outp[i].val)

    _assemble(nc, st, sems, prog)
    return nc


def kernel(L0_attn_x, L0_attn_w1, L0_attn_w2, L0_attn_b2, L0_attn_sparse_mask,
           L0_ffn_U1, L0_ffn_V1, L0_ffn_U2, L0_ffn_V2, L0_ffn_b1, L0_ffn_b2):
    x = np.ascontiguousarray(np.asarray(L0_attn_x, dtype=np.float32).reshape(2 * D, D))
    w1 = np.asarray(L0_attn_w1, dtype=np.float32)
    w2 = np.ascontiguousarray(np.asarray(L0_attn_w2, dtype=np.float32))
    b2 = np.asarray(L0_attn_b2, dtype=np.float32)
    mask = np.asarray(L0_attn_sparse_mask, dtype=np.float32)
    U1 = np.asarray(L0_ffn_U1, dtype=np.float32)
    V1 = np.asarray(L0_ffn_V1, dtype=np.float32)
    U2 = np.ascontiguousarray(np.asarray(L0_ffn_U2, dtype=np.float32))
    V2 = np.asarray(L0_ffn_V2, dtype=np.float32)
    b1 = np.asarray(L0_ffn_b1, dtype=np.float32)
    b2f = np.asarray(L0_ffn_b2, dtype=np.float32)

    mt = _mask_support(mask)
    tiles = [(s, tc) for tc in range(NT) for s in sorted(np.nonzero(mt[:, tc])[0])]
    n_tiles = len(tiles)
    maskp = np.ascontiguousarray(np.stack(
        [mask[s * 128:(s + 1) * 128, t * 128:(t + 1) * 128] for (s, t) in tiles])) \
        if n_tiles else np.zeros((1, 128, 128), np.float32)

    w1q = np.ascontiguousarray(w1.reshape(NT, 128, NT, 128).transpose(2, 1, 0, 3))
    u1q = np.ascontiguousarray(U1.reshape(NT, 128, RANK).transpose(1, 0, 2))
    v1q = np.ascontiguousarray(V1.reshape(NR, 128, 2 * INTER).transpose(1, 0, 2))
    v2q = np.ascontiguousarray(V2.reshape(NR, 128, D).transpose(1, 0, 2))
    b1g = b1[:INTER].reshape(NI, 128).T
    b1u = b1[INTER:].reshape(NI, 128).T
    b1p = np.ascontiguousarray(np.concatenate([b1g, b1u], axis=1))
    b2r = np.ascontiguousarray(b2.reshape(1, D))
    b2fr = np.ascontiguousarray(b2f.reshape(1, D))
    ident = np.eye(128, dtype=np.float32)

    nc = build_program(mt)
    in_maps = []
    for c in range(NCORES):
        in_maps.append({
            "x": x[c * TOK:(c + 1) * TOK],
            "w1q": w1q, "w2": w2, "maskp": maskp,
            "u1q": u1q, "v1q": v1q, "u2": U2, "v2q": v2q,
            "b1p": b1p, "b2r": b2r, "b2fr": b2fr, "ident": ident,
        })
    res = run_bass_kernel_spmd(nc, in_maps, core_ids=list(range(NCORES)))
    out = np.concatenate([np.asarray(r["out"]) for r in res.results], axis=0)
    return out.reshape(2, D, D).astype(np.float32)
